# version 01
# baseline (speedup 1.0000x reference)
import numpy as np

import concourse.bass as bass
import concourse.bacc as bacc
import concourse.mybir as mybir
from concourse.bass_utils import run_bass_kernel_spmd

F32 = mybir.dt.float32
I32 = mybir.dt.int32
AF = mybir.ActivationFunctionType
OP = mybir.AluOpType
AX = mybir.AxisListType

D = 2048
H = 16
DK = 128
T = 8192
NCORES = 8
TB = 512
NTT = TB // 128
NIC = D // 128
NCH = TB // 64
SLOT = H * (DK + 1)
PAY = SLOT + 1

_TAPS = []

ENGINES = ("sync", "tensor", "scalar", "vector", "gpsimd")


class Ctr:
    def __init__(self, name):
        self.name = name
        self.h = None
        self.n = 0


class Plan:
    def __init__(self):
        self.q = {e: [] for e in ENGINES}

    def emit(self, eng, fn):
        self.q[eng].append(fn)

    def wait(self, eng, sem, val):
        if val > 0:
            self.q[eng].append(lambda e, s=sem, v=val: e.wait_ge(s.h, v))


def _view(arena, off, shape):
    n = int(np.prod(shape))
    v = arena[:, off:off + n]
    if len(shape) > 1:
        names = " ".join(f"d{i}" for i in range(len(shape)))
        v = v.rearrange(f"p ({names}) -> p {names}",
                        **{f"d{i}": s for i, s in enumerate(shape)})
    return v


def build_program(taps=()):
    nc = bacc.Bacc("TRN2", target_bir_lowering=False, debug=False)

    xa = nc.dram_tensor("xa", [TB, D], F32, kind="ExternalInput").ap()
    xb = nc.dram_tensor("xb", [TB, D], F32, kind="ExternalInput").ap()
    wts = {w: nc.dram_tensor(f"w{w}", [NIC, D, 128], F32, kind="ExternalInput").ap()
           for w in ("q", "k", "v", "g", "o", "f")}
    bcol_d = nc.dram_tensor("bcol", [128, 96], F32, kind="ExternalInput").ap()
    rope_d = {"A": nc.dram_tensor("ropeA", [128, 2 * TB], F32, kind="ExternalInput").ap(),
              "B": nc.dram_tensor("ropeB", [128, 2 * TB], F32, kind="ExternalInput").ap()}
    pst_d = nc.dram_tensor("pst", [128, 128], F32, kind="ExternalInput").ap()
    id_d = nc.dram_tensor("ident", [128, 128], F32, kind="ExternalInput").ap()
    rank_d = nc.dram_tensor("rankc", [128, 1], F32, kind="ExternalInput").ap()
    y_d = nc.dram_tensor("y", [2 * TB, D], F32, kind="ExternalOutput").ap()
    thbm = nc.dram_tensor("thbm", [NIC, D, 128], F32, kind="Internal").ap()

    tap_d = {}
    for t_ in taps:
        tap_d[t_] = nc.dram_tensor(f"tap_{t_}", [128, 8448], F32,
                                   kind="ExternalOutput").ap()

    off = {}
    cur = 0
    def alloc(name, n):
        nonlocal cur
        off[name] = cur
        cur += n
    alloc("ident", 128)
    alloc("pst", 128)
    alloc("bcol", 96)
    alloc("rankc", 1)
    alloc("onescol", 1)
    alloc("onesrow", 128)
    alloc("rope", 2 * TB)
    alloc("L", PAY)
    alloc("P", SLOT)
    alloc("totA", SLOT)
    alloc("S", DK + 1)
    alloc("numsc", 128)
    alloc("misc", 64)
    alloc("yT", NIC * TB)
    alloc("kT", NTT * D)
    alloc("vX", NTT * H * (DK + 1))
    alloc("outT", NIC * TB)
    alloc("wslab", 2 * NIC * 128)
    alloc("stage", 3 * 2 * TB)
    alloc("qF", 2 * TB)
    alloc("gF", 2 * TB)
    alloc("xtile", D)
    assert cur <= 52900, cur
    assert 7 * PAY <= NIC * TB + 2 * NIC * 128 + 3 * 2 * TB

    from contextlib import ExitStack
    with ExitStack() as ctx:
        arena_t = ctx.enter_context(nc.sbuf_tensor("arena", [128, cur], F32))
        psums = [ctx.enter_context(nc.psum_tensor(f"ps{i}", [128, 512], F32))
                 for i in range(8)]
        P0, P1, P2, P3, P4, P5, P6, P7 = psums
        semh = {nm: ctx.enter_context(nc.semaphore("s_" + nm))
                for nm in ("cn", "xt", "w0", "w1", "fc", "xr", "sp",
                           "mm", "ac", "dv", "gp", "pr", "ls", "rs", "rd",
                           "st")}
        block = ctx.enter_context(nc.Block())

        arena = arena_t.ap()
        A = lambda name, shape: _view(arena, off[name], shape)

        ident = A("ident", (128,))
        pst = A("pst", (128,))
        bcol = A("bcol", (96,))
        rankc = A("rankc", (1,))
        onescol = A("onescol", (1,))
        onesrow = _view(arena, off["onesrow"], (128,))[0:1, :]
        rope = A("rope", (2, TB))
        Lb = A("L", (PAY,))
        Pb = A("P", (SLOT,))
        totA = A("totA", (SLOT,))
        S_sb = A("S", (DK + 1,))
        numsc = A("numsc", (128,))
        misc = A("misc", (64,))
        yT = A("yT", (NIC, TB))
        kT = A("kT", (NTT, D))
        vX = A("vX", (NTT, H, DK + 1))
        outT = A("outT", (NIC, TB))
        wslab = A("wslab", (2, NIC, 128))
        stage = A("stage", (3, 2, TB))
        qF = A("qF", (2, TB))
        gF = A("gF", (2, TB))
        xtile = A("xtile", (D,))
        xres = _view(arena, off["vX"], (NTT, D))
        x2 = xres
        x1 = kT
        slots = _view(arena, off["outT"], (7, PAY))
        wfchunk = _view(arena, off["outT"], (2, NIC, 128))
        tchunk = _view(arena, off["outT"] + 2 * D, (2, NIC, 128))
        uchunk = _view(arena, off["wslab"], (NIC, 128))
        sqdump1 = _view(arena, off["kT"], (D,))
        sqdump2 = _view(arena, off["stage"], (D,))
        wof = stage[:, 0, 0, :]
        ffs = stage[:, 0, 1, :]

        sems = {}
        for nm, h in semh.items():
            c = Ctr(nm)
            c.h = h
            sems[nm] = c
        cn, xt, fc, xr, sp = (sems[k] for k in ("cn", "xt", "fc", "xr", "sp"))
        wsem = (sems["w0"], sems["w1"])
        mm_, ac, dv, gp = (sems[k] for k in ("mm", "ac", "dv", "gp"))
        pr, ls, rs, rd_, st = (sems[k] for k in ("pr", "ls", "rs", "rd", "st"))

        plan = Plan()

        def sdma(eng, sem, out_ap, in_ap):
            plan.emit(eng, lambda e, o=out_ap, i=in_ap, s=sem:
                      e.dma_start(out=o, in_=i).then_inc(s.h, 16))
            sem.n += 16
            return sem.n

        def mm(psum_ap, lhsT, rhs, start, stop, inc=False, tile_position=None):
            def f(e, p=psum_ap, l=lhsT, r=rhs, st_=start, sp=stop,
                  tp=tile_position, i=inc):
                ins = e.matmul(p, l, r, start=st_, stop=sp, tile_position=tp)
                if i:
                    ins.then_inc(mm_.h, 1)
            plan.emit("tensor", f)
            if inc:
                mm_.n += 1
            return mm_.n

        def transpose(psum_ap, in_ap):
            plan.emit("tensor", lambda e, p=psum_ap, i=in_ap:
                      e.transpose(p, i, ident).then_inc(mm_.h, 1))
            mm_.n += 1
            return mm_.n

        def act(fn):
            plan.wait("scalar", ac, ac.n)
            plan.emit("scalar", lambda e, f=fn: f(e).then_inc(ac.h, 1))
            ac.n += 1
            return ac.n

        def dve(fn):
            plan.wait("vector", dv, dv.n)
            plan.emit("vector", lambda e, f=fn: f(e).then_inc(dv.h, 1))
            dv.n += 1
            return dv.n

        def twait(sem, val):
            plan.wait("tensor", sem, val)

        def tap(name, src_ap, marks):
            if name not in tap_d:
                return
            n = int(np.prod(src_ap.shape[1:]))
            npart = src_ap.shape[0]
            for semc, v in marks:
                plan.wait("scalar", semc, v)
            dst = tap_d[name][0:npart, 0:n]
            plan.emit("scalar", lambda e, o=dst, i=src_ap:
                      e.dma_start(out=o, in_=i).then_inc(sp.h, 16))
            sp.n += 16
            for eng in ENGINES:
                plan.wait(eng, sp, sp.n)

        sdma("sync", cn, A("ident", (128,)), id_d[:, :])
        sdma("sync", cn, pst, pst_d[:, :])
        sdma("sync", cn, bcol, bcol_d[:, :])
        sdma("sync", cn, rankc, rank_d[:, :])
        ld_const = cn.n
        plan.wait("vector", cn, ld_const)
        dve(lambda e: e.memset(onescol, 1.0))
        dv_const = dve(lambda e: e.memset(onesrow, 1.0))

        WIDX = {"q": 0, "k": 1, "v": 2, "g": 3, "o": 4, "f": 5}
        def bc(w, oc):
            return bcol[:, WIDX[w] * 16 + oc: WIDX[w] * 16 + oc + 1]

        bankfree = {id(b): None for b in (P0, P1, P2, P3, P4, P5, P6, P7)}
        tr_state = {"i": 0}
        def tr_bank():
            b = (P6, P7)[tr_state["i"] % 2]
            tr_state["i"] += 1
            return b

        def proj_mms(psum, slabv, src, ld_mark, prev_free):
            twait(ld_mark[0], ld_mark[1])
            if prev_free is not None:
                twait(*prev_free)
            for ic in range(NIC):
                mm(psum[:, 0:TB], slabv[:, ic, :], src(ic),
                   start=(ic == 0), stop=(ic == NIC - 1), inc=(ic == NIC - 1))
            return mm_.n

        def slab_load(src_d, oc, par, prev_use_mark):
            if prev_use_mark is not None:
                plan.wait("sync", mm_, prev_use_mark)
            s = wsem[par]
            return (s, sdma("sync", s, wslab[:, par, :, :],
                            src_d[oc, :, :].rearrange("(ic p) o -> p ic o",
                                                      p=128)))

        s1 = misc[:, 0:1]; s2 = misc[:, 1:2]; mcol = misc[:, 2:3]
        varc = misc[:, 3:4]; rstd = misc[:, 4:5]

        def ln_to_fm(src_tile, tt, dump, ready):
            for eng, semc, v in ready:
                plan.wait(eng, semc, v)
            dve(lambda e, x=src_tile: e.reduce_sum(out=s1, in_=x, axis=AX.X))
            a_sq = act(lambda e, x=src_tile, d=dump: e.activation(
                out=d, in_=x, func=AF.Square, accum_out=s2))
            dve(lambda e: e.tensor_scalar(out=mcol, in0=s1, scalar1=1.0 / D,
                                          scalar2=None, op0=OP.mult))
            plan.wait("vector", ac, a_sq)
            dve(lambda e: e.tensor_scalar(out=varc, in0=s2, scalar1=1.0 / D,
                                          scalar2=None, op0=OP.mult))
            dve(lambda e: e.scalar_tensor_tensor(
                out=varc, in0=mcol, scalar=mcol, in1=varc,
                op0=OP.mult, op1=OP.subtract))
            d_var = dve(lambda e: e.tensor_scalar(out=varc, in0=varc, scalar1=-1.0,
                                                  scalar2=1e-5, op0=OP.mult,
                                                  op1=OP.add))
            plan.wait("scalar", dv, d_var)
            a_sqrt = act(lambda e: e.activation(out=rstd, in_=varc, func=AF.Sqrt))
            plan.wait("vector", ac, a_sqrt)
            dve(lambda e: e.reciprocal(out=rstd, in_=rstd))
            d_y = dve(lambda e, x=src_tile, d=dump: e.tensor_scalar(
                out=d, in0=x, scalar1=mcol, scalar2=rstd,
                op0=OP.subtract, op1=OP.mult))
            for ic in range(NIC):
                b = tr_bank()
                if bankfree[id(b)] is not None:
                    twait(*bankfree[id(b)])
                twait(dv, d_y)
                m = transpose(b[:, 0:128], dump[:, ic * 128:(ic + 1) * 128])
                plan.wait("scalar", mm_, m)
                a = act(lambda e, bb=b, ic_=ic, tt_=tt: e.copy(
                    out=yT[:, ic_, tt_ * 128:(tt_ + 1) * 128], in_=bb[:, 0:128]))
                bankfree[id(b)] = (ac, a)
            return d_y

        rope_mark = [0]

        def rope_elu(psum, w, oc, par, dst, stage_free):
            kfs = stage[:, 0, par, :]
            ta = stage[:, 1, par, :]
            tb = stage[:, 2, par, :]
            m_psum = mm_.n
            plan.wait("scalar", mm_, m_psum)
            if stage_free is not None:
                plan.wait("scalar", *stage_free)
            a_evac = act(lambda e, p=psum, w_=w, oc_=oc: e.activation(
                out=kfs, in_=p[:, 0:TB], func=AF.Identity, bias=bc(w_, oc_),
                scale=1.0))
            if bankfree[id(P2)] is not None:
                twait(*bankfree[id(P2)])
            twait(ac, a_evac)
            m_ps = mm(P2[:, 0:TB], pst, kfs, start=True, stop=True, inc=True)
            plan.wait("vector", ac, a_evac)
            plan.wait("vector", cn, rope_mark[0])
            if stage_free is not None:
                plan.wait("vector", *stage_free)
            dve(lambda e: e.tensor_tensor(out=ta, in0=kfs, in1=rope[:, 0, :],
                                          op=OP.mult))
            plan.wait("vector", mm_, m_ps)
            d_tb = dve(lambda e: e.tensor_tensor(out=tb, in0=P2[:, 0:TB],
                                                 in1=rope[:, 1, :], op=OP.mult))
            bankfree[id(P2)] = (dv, d_tb)
            dve(lambda e: e.tensor_tensor(out=kfs, in0=ta, in1=tb, op=OP.add))
            d_min = dve(lambda e: e.tensor_scalar(out=ta, in0=kfs, scalar1=0.0,
                                                  scalar2=None, op0=OP.min))
            plan.wait("scalar", dv, d_min)
            a_exp = act(lambda e: e.activation(out=tb, in_=ta, func=AF.Exp))
            plan.wait("vector", ac, a_exp)
            d_out = dve(lambda e, o=dst: e.scalar_tensor_tensor(
                out=o, in0=kfs, scalar=0.0, in1=tb, op0=OP.max, op1=OP.add))
            return d_out

        for phase, (x_hbm, blk) in enumerate(((xa, "A"), (xb, "B"))):
            if phase == 1:
                barrier = [(cn, cn.n), (xt, xt.n), (wsem[0], wsem[0].n),
                           (wsem[1], wsem[1].n), (fc, fc.n), (xr, xr.n),
                           (sp, sp.n), (mm_, mm_.n), (ac, ac.n),
                           (dv, dv.n), (st, st.n)]
                for eng in ENGINES:
                    for semc, v in barrier:
                        plan.wait(eng, semc, v)

            plan.wait("sync", dv, dv_const)
            ld_rope = sdma("sync", cn, rope, rope_d[blk][:, :].rearrange(
                "p (a t) -> p a t", a=2))
            rope_mark[0] = ld_rope

            d_ones = dve(lambda e: e.memset(vX[:, :, :, DK:DK + 1], 1.0))

            d_lnlast = None
            for tt in range(NTT):
                plan.wait("sync", dv, dv.n)
                plan.wait("sync", ac, ac.n)
                ldx = sdma("sync", xt, xtile, x_hbm[tt * 128:(tt + 1) * 128, :])
                ready = [("vector", xt, ldx), ("scalar", xt, ldx)]
                d_lnlast = ln_to_fm(xtile, tt, sqdump1, ready)
            a_yT = ac.n
            tap(f"yT_{blk}", yT, [(ac, a_yT)])

            twait(ac, a_yT)
            prev_slab_mark = None
            prev_stage = None
            for wname in ("k", "v"):
                for oc in range(H):
                    par = oc % 2
                    ld_mark = slab_load(wts[wname], oc, par, prev_slab_mark)
                    psum = (P0, P1)[par]
                    m_proj = proj_mms(psum, wslab[:, par], lambda ic: yT[:, ic, :],
                                      ld_mark, bankfree[id(psum)])
                    prev_slab_mark = m_proj
                    if wname == "k":
                        d_kr = rope_elu(psum, "k", oc, par, stage[:, 0, par, :],
                                        prev_stage)
                        src_t = stage[:, 0, par, :]
                        dep = (dv, d_kr)
                        bankfree[id(psum)] = (ac, ac.n)
                    else:
                        plan.wait("scalar", mm_, m_proj)
                        if prev_stage is not None:
                            plan.wait("scalar", *prev_stage)
                        a_ev = act(lambda e, p=psum, oc_=oc, pr_=par: e.activation(
                            out=stage[:, 0, pr_, :], in_=p[:, 0:TB],
                            func=AF.Identity, bias=bc("v", oc_), scale=1.0))
                        src_t = stage[:, 0, par, :]
                        dep = (ac, a_ev)
                        bankfree[id(psum)] = (ac, a_ev)
                    last_a = None
                    for tt in range(NTT):
                        b = tr_bank()
                        if bankfree[id(b)] is not None:
                            twait(*bankfree[id(b)])
                        twait(*dep)
                        m = transpose(b[:, 0:128],
                                      src_t[:, tt * 128:(tt + 1) * 128])
                        plan.wait("scalar", mm_, m)
                        if wname == "k":
                            dst = kT[:, tt, oc * 128:(oc + 1) * 128]
                        else:
                            plan.wait("scalar", dv, d_ones)
                            dst = vX[:, tt, oc, 0:DK]
                        a = act(lambda e, bb=b, d_=dst: e.copy(out=d_,
                                                               in_=bb[:, 0:128]))
                        bankfree[id(b)] = (ac, a)
                        last_a = a
                    prev_stage = (ac, last_a)
            a_kv_done = ac.n
            m_kv_done = mm_.n
            tap(f"kT_{blk}", kT, [(ac, a_kv_done)])
            tap(f"vX_{blk}", vX, [(ac, a_kv_done)])

            plan.wait("scalar", ls, 112 * phase)
            plan.wait("vector", ls, 112 * phase)
            for h in range(H):
                psum = (P3, P4)[h % 2]
                if bankfree[id(psum)] is not None:
                    twait(*bankfree[id(psum)])
                twait(ac, a_kv_done)
                for tt in range(NTT):
                    mm(psum[:, 0:DK + 1], kT[:, tt, h * 128:(h + 1) * 128],
                       vX[:, tt, h, :], start=(tt == 0), stop=(tt == NTT - 1),
                       inc=(tt == NTT - 1))
                m = mm_.n
                plan.wait("scalar", mm_, m)
                a = act(lambda e, p=psum, h_=h: e.copy(
                    out=Lb[:, h_ * (DK + 1):(h_ + 1) * (DK + 1)],
                    in_=p[:, 0:DK + 1]))
                bankfree[id(psum)] = (ac, a)
            plan.wait("vector", cn, ld_const)
            d_rank = dve(lambda e: e.tensor_copy(out=Lb[:, SLOT:PAY], in_=rankc))
            a_L = ac.n
            tap(f"L_{blk}", Lb, [(ac, a_L), (dv, d_rank)])

            plan.wait("gpsimd", ac, a_kv_done)
            plan.wait("gpsimd", mm_, m_kv_done)
            plan.emit("gpsimd", lambda e: e.remote_sem_update_broadcast(
                remote_sem=rd_.h, local_sem=gp.h,
                rdests=[None] + [(0, j) for j in range(1, 8)]).then_inc(pr.h, 1))
            pr.n += 1
            plan.wait("gpsimd", pr, pr.n)
            plan.emit("gpsimd", lambda e: e.trigger_dma(count=1))
            plan.wait("gpsimd", rd_, 14 * (phase + 1))
            plan.wait("gpsimd", ac, a_L)
            plan.wait("gpsimd", dv, d_rank)
            for j in range(1, 8):
                rr = [None] * 8
                rr[j] = (0, j)
                plan.emit("gpsimd", lambda e, j_=j, r_=tuple(rr):
                          e.remote_dma_broadcast(
                              out_ap=slots[:, j_ - 1, :], in_ap=Lb[:, :],
                              remote_sem=rs.h, local_sem=ls.h,
                              rdests=list(r_)).then_inc(pr.h, 1))
                pr.n += 1
            plan.wait("gpsimd", pr, pr.n)
            plan.emit("gpsimd", lambda e: e.trigger_dma(count=7))
            ls.n += 7 * 16
            plan.wait("vector", rs, 14 * (phase + 1))
            plan.wait("vector", ac, a_L)
            mk = misc[:, 8:9]
            dve(lambda e: e.memset(Pb[:, :], 0.0))
            for j in range(7):
                dve(lambda e, j_=j: e.tensor_tensor(
                    out=mk, in0=slots[:, j_, SLOT:PAY], in1=rankc, op=OP.is_lt))
                dve(lambda e, j_=j: e.scalar_tensor_tensor(
                    out=Pb[:, :], in0=slots[:, j_, 0:SLOT], scalar=mk,
                    in1=Pb[:, :], op0=OP.mult, op1=OP.add))
            if phase == 0:
                dve(lambda e: e.tensor_copy(out=totA[:, :], in_=Lb[:, 0:SLOT]))
                for j in range(7):
                    dve(lambda e, j_=j: e.tensor_tensor(
                        out=totA[:, :], in0=totA[:, :], in1=slots[:, j_, 0:SLOT],
                        op=OP.add))
                dve(lambda e: e.tensor_copy(out=totA[:, :].bitcast(I32),
                                            in_=totA[:, :].bitcast(I32)))
            else:
                dve(lambda e: e.tensor_tensor(out=Pb[:, :], in0=Pb[:, :],
                                              in1=totA[:, :], op=OP.add))
            d_fold = dve(lambda e: e.tensor_copy(out=Pb[:, :].bitcast(I32),
                                                 in_=Pb[:, :].bitcast(I32)))
            tap(f"P_{blk}", Pb, [(dv, d_fold)])

            jit_state = {("stagefree", -2): (dv, d_fold),
                         ("stagefree", -1): (dv, d_fold),
                         ("gfree", -2): (dv, d_fold),
                         ("gfree", -1): (dv, d_fold)}
            plan.wait("sync", dv, d_fold)

            def emit_jit(h):
                par = h % 2
                ld_q = slab_load(wts["q"], h, par, jit_state.get(("slabmark", h - 2)))
                m_q = proj_mms(P0, wslab[:, par], lambda ic: yT[:, ic, :],
                               ld_q, bankfree[id(P0)])
                d_q = rope_elu(P0, "q", h, par, qF[:, par, :],
                               jit_state.get(("stagefree", h - 2)))
                bankfree[id(P0)] = (ac, ac.n)
                ld_g = slab_load(wts["g"], h, par, m_q)
                m_g = proj_mms(P1, wslab[:, par], lambda ic: yT[:, ic, :],
                               ld_g, bankfree[id(P1)])
                plan.wait("scalar", mm_, m_g)
                prevg = jit_state.get(("gfree", h - 2))
                if prevg is not None:
                    plan.wait("scalar", *prevg)
                a_g = act(lambda e, h_=h, pr_=par: e.activation(
                    out=gF[:, pr_, :], in_=P1[:, 0:TB], func=AF.Sigmoid,
                    bias=bc("g", h_), scale=1.0))
                bankfree[id(P1)] = (ac, a_g)
                jit_state[("slabmark", h)] = m_g
                jit_state[("q", h)] = d_q
                jit_state[("g", h)] = a_g

            emit_jit(0)
            emit_jit(1)
            dn = misc[:, 16:17]
            rcp = misc[:, 17:18]
            for h in range(H):
                par = h % 2
                d_S = dve(lambda e, h_=h: e.tensor_copy(
                    out=S_sb, in_=Pb[:, h_ * (DK + 1):(h_ + 1) * (DK + 1)]))
                for c in range(NCH):
                    tt, half = divmod(c, 2)
                    pb = (P3, P4)[c % 2]
                    if bankfree[id(pb)] is not None:
                        twait(*bankfree[id(pb)])
                    m1 = mm(pb[:, 0:DK + 1],
                            kT[64 * half:64 * (half + 1), tt,
                               h * 128:(h + 1) * 128],
                            vX[64 * half:64 * (half + 1), tt, h, :],
                            start=True, stop=True, inc=True)
                    plan.wait("vector", mm_, m1)
                    d_S = dve(lambda e, p=pb: e.tensor_tensor(
                        out=S_sb, in0=S_sb, in1=p[:, 0:DK + 1], op=OP.add))
                    bankfree[id(pb)] = (dv, d_S)
                    if half == 0 and bankfree[id(P5)] is not None:
                        twait(*bankfree[id(P5)])
                    twait(dv, d_S)
                    twait(dv, jit_state[("q", h)])
                    m2 = mm(P5[64 * half:64 * (half + 1), 0:DK + 1],
                            qF[:, par, c * 64:(c + 1) * 64], S_sb,
                            start=True, stop=True, inc=True,
                            tile_position=(0, 64 * half))
                    if half == 1:
                        plan.wait("vector", mm_, m2)
                        dve(lambda e: e.tensor_scalar(
                            out=dn, in0=P5[:, DK:DK + 1], scalar1=1e-6,
                            scalar2=None, op0=OP.add))
                        dve(lambda e: e.reciprocal(out=rcp, in_=dn))
                        d_ev = dve(lambda e: e.tensor_scalar(
                            out=numsc, in0=P5[:, 0:DK], scalar1=rcp,
                            scalar2=None, op0=OP.mult))
                        bankfree[id(P5)] = (dv, d_ev)
                        b = tr_bank()
                        if bankfree[id(b)] is not None:
                            twait(*bankfree[id(b)])
                        twait(dv, d_ev)
                        mtr = transpose(b[:, 0:128], numsc)
                        plan.wait("vector", mm_, mtr)
                        plan.wait("vector", ac, jit_state[("g", h)])
                        d_g = dve(lambda e, bb=b, h_=h, tt_=tt, pr_=par:
                                  e.tensor_tensor(
                                      out=outT[:, h_, tt_ * 128:(tt_ + 1) * 128],
                                      in0=bb[:, 0:128],
                                      in1=gF[:, pr_, tt_ * 128:(tt_ + 1) * 128],
                                      op=OP.mult))
                        bankfree[id(b)] = (dv, d_g)
                if h + 2 < H:
                    jit_state[("stagefree", h)] = (mm_, mm_.n)
                    jit_state[("gfree", h)] = (dv, dv.n)
                    emit_jit(h + 2)
            d_scan_done = dv.n
            tap(f"outT_{blk}", outT, [(dv, d_scan_done)])

            plan.wait("scalar", dv, d_scan_done)
            for tt in range(NTT):
                sdma("scalar", xr, xres[:, tt, :],
                     x_hbm[tt * 128:(tt + 1) * 128, :])
            xr_all = xr.n
            prev_slab_mark = mm_.n
            prev_wof = None
            d_x1_done = None
            for oc in range(NIC):
                par = oc % 2
                ldm = slab_load(wts["o"], oc, par, prev_slab_mark)
                psum = (P0, P1)[par]
                twait(dv, d_scan_done)
                m_proj = proj_mms(psum, wslab[:, par], lambda ic: outT[:, ic, :],
                                  ldm, bankfree[id(psum)])
                prev_slab_mark = m_proj
                plan.wait("scalar", mm_, m_proj)
                if prev_wof is not None:
                    plan.wait("scalar", *prev_wof)
                a_ev = act(lambda e, p=psum, oc_=oc: e.activation(
                    out=wof, in_=p[:, 0:TB], func=AF.Identity, bias=bc("o", oc_),
                    scale=1.0))
                bankfree[id(psum)] = (ac, a_ev)
                last = None
                for tt in range(NTT):
                    b = tr_bank()
                    if bankfree[id(b)] is not None:
                        twait(*bankfree[id(b)])
                    twait(ac, a_ev)
                    mtr = transpose(b[:, 0:128], wof[:, tt * 128:(tt + 1) * 128])
                    plan.wait("vector", mm_, mtr)
                    plan.wait("vector", xr, xr_all)
                    d_x1 = dve(lambda e, bb=b, tt_=tt, oc_=oc: e.tensor_tensor(
                        out=x1[:, tt_, oc_ * 128:(oc_ + 1) * 128],
                        in0=bb[:, 0:128],
                        in1=xres[:, tt_, oc_ * 128:(oc_ + 1) * 128], op=OP.add))
                    bankfree[id(b)] = (dv, d_x1)
                    last = d_x1
                prev_wof = (mm_, mm_.n)
                d_x1_done = last
            d_x1_done = dv.n
            tap(f"x1_{blk}", x1, [(dv, d_x1_done)])

            for tt in range(NTT):
                ready = [("vector", dv, d_x1_done), ("scalar", dv, d_x1_done)]
                ln_to_fm(x1[:, tt, :], tt, sqdump2, ready)
            a_y2 = ac.n
            tap(f"y2T_{blk}", yT, [(ac, a_y2)])

            acc16 = misc[:, 32:48]
            scale_c = misc[:, 22:23]
            if phase == 0:
                for ic in range(NIC):
                    par = ic % 2
                    plan.wait("sync", mm_, prev_slab_mark)
                    plan.wait("sync", ac, ac.n)
                    ldc = sdma("sync", fc, wfchunk[:, par, :, :],
                               wts["f"][ic, :, :].rearrange(
                                   "(icp p) o -> p icp o", p=128))
                    plan.wait("scalar", fc, ldc)
                    plan.wait("scalar", mm_, prev_slab_mark)
                    act(lambda e, par_=par, ic_=ic: e.activation(
                        out=uchunk, in_=wfchunk[:, par_, :, :], func=AF.Abs,
                        accum_out=acc16[:, ic_:ic_ + 1]))
                a_abs = ac.n
                plan.wait("vector", ac, a_abs)
                d_red = dve(lambda e: e.reduce_sum(out=misc[:, 20:21], in_=acc16,
                                                   axis=AX.X))
                if bankfree[id(P2)] is not None:
                    twait(*bankfree[id(P2)])
                twait(dv, d_red)
                m_red = mm(P2[0:1, 0:1], misc[:, 20:21], onescol, start=True,
                           stop=True, inc=True)
                plan.wait("scalar", mm_, m_red)
                a_sc1 = act(lambda e: e.copy(out=misc[0:1, 21:22],
                                             in_=P2[0:1, 0:1]))
                twait(ac, a_sc1)
                m_bc = mm(P2[:, 1:2], onesrow, misc[0:1, 21:22], start=True,
                          stop=True, inc=True)
                plan.wait("vector", mm_, m_bc)
                sinv_c = misc[:, 23:24]
                dve(lambda e: e.tensor_scalar(out=scale_c, in0=P2[:, 1:2],
                                              scalar1=1.0 / (D * D), scalar2=None,
                                              op0=OP.mult))
                bankfree[id(P2)] = (dv, dv.n)
                dve(lambda e: e.tensor_scalar(out=sinv_c, in0=scale_c,
                                              scalar1=1e-10, scalar2=None,
                                              op0=OP.add))
                d_sc = dve(lambda e: e.reciprocal(out=sinv_c, in_=sinv_c))
                tap("scale", misc, [(dv, d_sc)])
                prev_spill = None
                for ic in range(NIC):
                    par = ic % 2
                    plan.wait("sync", dv, dv.n)
                    ldc = sdma("sync", fc, wfchunk[:, par, :, :],
                               wts["f"][ic, :, :].rearrange(
                                   "(icp p) o -> p icp o", p=128))
                    plan.wait("vector", fc, ldc)
                    plan.wait("vector", dv, d_sc)
                    if prev_spill is not None:
                        plan.wait("vector", sp, prev_spill)
                    dve(lambda e, p_=par: e.tensor_scalar(
                        out=uchunk, in0=wfchunk[:, p_, :, :], scalar1=sinv_c,
                        scalar2=None, op0=OP.mult))
                    dve(lambda e, p_=par: e.tensor_scalar(
                        out=tchunk[:, p_, :, :], in0=uchunk, scalar1=0.5,
                        scalar2=None, op0=OP.is_ge))
                    dve(lambda e: e.tensor_scalar(
                        out=uchunk, in0=uchunk, scalar1=-0.5, scalar2=None,
                        op0=OP.is_le))
                    d_t = dve(lambda e, p_=par: e.tensor_tensor(
                        out=tchunk[:, p_, :, :], in0=tchunk[:, p_, :, :],
                        in1=uchunk, op=OP.subtract))
                    plan.wait("scalar", dv, d_t)
                    prev_spill = sdma("scalar", sp,
                                      thbm[ic, :, :].rearrange(
                                          "(icp p) o -> p icp o", p=128),
                                      tchunk[:, par, :, :])
                plan.wait("sync", sp, prev_spill)

            prev_slab_mark2 = None if phase == 0 else mm_.n
            prev_ff = None
            for oc in range(NIC):
                par = oc % 2
                if phase == 0 and oc < 2:
                    plan.wait("sync", sp, sp.n)
                ldm = slab_load(thbm, oc, par, prev_slab_mark2)
                psum = (P0, P1)[par]
                plan.wait("tensor", ac, a_y2)
                m_proj = proj_mms(psum, wslab[:, par], lambda ic: yT[:, ic, :],
                                  ldm, bankfree[id(psum)])
                prev_slab_mark2 = m_proj
                plan.wait("scalar", mm_, m_proj)
                if prev_ff is not None:
                    plan.wait("scalar", *prev_ff)
                a_ev = act(lambda e, p=psum, oc_=oc: e.activation(
                    out=ffs, in_=p[:, 0:TB], func=AF.Identity, bias=bc("f", oc_),
                    scale=scale_c))
                bankfree[id(psum)] = (ac, a_ev)
                for tt in range(NTT):
                    b = tr_bank()
                    if bankfree[id(b)] is not None:
                        twait(*bankfree[id(b)])
                    twait(ac, a_ev)
                    mtr = transpose(b[:, 0:128], ffs[:, tt * 128:(tt + 1) * 128])
                    plan.wait("vector", mm_, mtr)
                    d_o = dve(lambda e, bb=b, tt_=tt, oc_=oc: e.tensor_tensor(
                        out=x2[:, tt_, oc_ * 128:(oc_ + 1) * 128],
                        in0=bb[:, 0:128],
                        in1=x1[:, tt_, oc_ * 128:(oc_ + 1) * 128], op=OP.add))
                    bankfree[id(b)] = (dv, d_o)
                prev_ff = (mm_, mm_.n)
            for tt in range(NTT):
                plan.wait("scalar", dv, dv.n)
                sdma("scalar", st,
                     y_d[phase * TB + tt * 128: phase * TB + (tt + 1) * 128, :],
                     x2[:, tt, :])

        plan.wait("gpsimd", st, st.n)
        plan.wait("gpsimd", sp, sp.n)

        @block.sync
        def _(e):
            for f in plan.q["sync"]:
                f(e)

        @block.tensor
        def _(e):
            for f in plan.q["tensor"]:
                f(e)

        @block.scalar
        def _(e):
            for f in plan.q["scalar"]:
                f(e)

        @block.vector
        def _(e):
            for f in plan.q["vector"]:
                f(e)

        @block.gpsimd
        def _(e):
            for f in plan.q["gpsimd"]:
                f(e)

    nc.finalize()
    return nc


def _host_inputs(L0_attn_x, Wq, bq, Wk, bk, Wv, bv, Wg, bg, Wo, bo, Wf, bf):
    x = np.ascontiguousarray(np.asarray(L0_attn_x, dtype=np.float32).reshape(T, D))
    def slabs(W):
        wt = np.asarray(W, dtype=np.float32).T
        return np.ascontiguousarray(
            wt.reshape(D, NIC, 128).transpose(1, 0, 2))
    ws = {k: slabs(v) for k, v in
          (("q", Wq), ("k", Wk), ("v", Wv), ("g", Wg), ("o", Wo), ("f", Wf))}
    bcol = np.zeros((128, 96), dtype=np.float32)
    for wi, b in enumerate((bq, bk, bv, bg, bo, bf)):
        b = np.asarray(b, dtype=np.float32)
        bcol[:, wi * 16:(wi + 1) * 16] = b.reshape(16, 128).T
    inv = (1.0 / 10000.0) ** (np.arange(0, DK, 2, dtype=np.float64) / DK)
    def rope_fm(pos):
        ang = np.outer(pos.astype(np.float64), inv)
        c = np.cos(ang).astype(np.float32)
        s = np.sin(ang).astype(np.float32)
        cf = np.repeat(c, 2, axis=1).T
        sf = np.repeat(s, 2, axis=1).T
        out = np.empty((128, 2 * TB), dtype=np.float32)
        out[:, 0:TB] = cf
        out[:, TB:2 * TB] = sf
        return out
    pst = np.zeros((128, 128), dtype=np.float32)
    for j in range(DK // 2):
        pst[2 * j + 1, 2 * j] = -1.0
        pst[2 * j, 2 * j + 1] = 1.0
    ident = np.eye(128, dtype=np.float32)
    in_maps = []
    for c in range(NCORES):
        posA = np.arange(c * TB, (c + 1) * TB)
        posB = np.arange(T // 2 + c * TB, T // 2 + (c + 1) * TB)
        m = {
            "xa": np.ascontiguousarray(x[c * TB:(c + 1) * TB]),
            "xb": np.ascontiguousarray(x[T // 2 + c * TB: T // 2 + (c + 1) * TB]),
            "bcol": bcol, "pst": pst, "ident": ident,
            "rankc": np.full((128, 1), float(c), dtype=np.float32),
            "ropeA": rope_fm(posA), "ropeB": rope_fm(posB),
        }
        for k, v in ws.items():
            m[f"w{k}"] = v
        in_maps.append(m)
    return in_maps


def kernel(L0_attn_x, L0_attn_Wq, L0_attn_bq, L0_attn_Wk, L0_attn_bk,
           L0_attn_Wv, L0_attn_bv, L0_attn_Wg, L0_attn_bg, L0_attn_Wo,
           L0_attn_bo, L0_ffn_weight, L0_ffn_bias):
    in_maps = _host_inputs(L0_attn_x, L0_attn_Wq, L0_attn_bq, L0_attn_Wk,
                           L0_attn_bk, L0_attn_Wv, L0_attn_bv, L0_attn_Wg,
                           L0_attn_bg, L0_attn_Wo, L0_attn_bo, L0_ffn_weight,
                           L0_ffn_bias)
    nc = build_program(taps=tuple(_TAPS))
    res = run_bass_kernel_spmd(nc, in_maps, core_ids=list(range(NCORES)))
    out = np.empty((T, D), dtype=np.float32)
    for c in range(NCORES):
        y = res.results[c]["y"]
        out[c * TB:(c + 1) * TB] = y[0:TB]
        out[T // 2 + c * TB: T // 2 + (c + 1) * TB] = y[TB:2 * TB]
    if _TAPS:
        kernel.tap_results = [{k: v for k, v in r.items() if k.startswith("tap_")}
                              for r in res.results]
    return out.reshape(1, T, D)
